# version 05 (final)
# baseline (speedup 1.0000x reference)
import os
import numpy as np
import ml_dtypes
from contextlib import ExitStack

MK_DEBUG = os.environ.get("MK_DEBUG", "0") == "1"
MK_BAR = int(os.environ.get("MK_BAR", "0"))
MK_NOBG = os.environ.get("MK_NOBG", "0") == "1"
MK_DEPTH1 = os.environ.get("MK_DEPTH1", "0") == "1"

import concourse.bass as bass
import concourse.mybir as mybir
from concourse.bass_utils import run_bass_kernel_spmd

AF = mybir.ActivationFunctionType
ALU = mybir.AluOpType
AX = mybir.AxisListType
F32 = mybir.dt.float32
BF = mybir.dt.bfloat16
F16 = mybir.dt.float16

D = 2048
S = 8192
T = 1024
NCORE = 8
H = 16
HD = 128
FF = 8192
NB = 8
DC = D // 128
FC = FF // 128
MAGIC = 12582912.0
INV2PI = float(1.0 / (2.0 * np.pi))
TWOPI = float(2.0 * np.pi)
SSCALE = float(1.0 / np.sqrt(128.0))
EXPB = -30.0
EPS = 1e-5

ENGINES = ("sync", "gpsimd", "scalar", "vector", "tensor")


class KB:

    def __init__(self, nc, stack):
        self.nc = nc
        self.prog = {e: [] for e in ENGINES}
        self.sems = {}
        self.ct = {}
        for s in ("sy", "sc", "gd", "pe", "ac", "dv", "po",
                  "t0", "r0", "r1", "s0", "s1", "g0", "g1",
                  "q0", "q1", "f0", "f1", "w0"):
            self.sems[s] = stack.enter_context(nc.semaphore(s))
            self.ct[s] = 0

    def op(self, eng, fn, inc=None, k=1):
        semh = None
        if inc is not None:
            self.ct[inc] += k
            semh = self.sems[inc]

        def run(e, fn=fn, semh=semh, k=k):
            ins = fn(e)
            if semh is not None:
                ins.then_inc(semh, k)
        self.prog[eng].append(run)
        return self.ct.get(inc)

    def dma(self, eng, out, in_, inc, **kw):
        return self.op(eng, lambda e, out=out, in_=in_, kw=kw:
                       e.dma_start(out=out, in_=in_, **kw), inc=inc, k=16)

    def wait(self, eng, sem, val=None):
        v = self.ct[sem] if val is None else val
        if v is None or v <= 0:
            return
        self.prog[eng].append(lambda e, s=self.sems[sem], v=v: e.wait_ge(s, v))

    def barrier(self):
        snap = dict(self.ct)
        for eng in ENGINES:
            for sname, v in snap.items():
                if v > 0:
                    self.wait(eng, sname, v)

    def emit(self):
        with self.nc.Block() as blk:
            for name in ENGINES:
                if self.prog[name]:
                    getattr(blk, name)(lambda e, fns=self.prog[name]: [f(e) for f in fns])


def build_nc():
    nc = bass.Bass("TRN2", target_bir_lowering=False)

    def inp(name, shape):
        return nc.dram_tensor(name, shape, F32, kind="ExternalInput").ap()

    xp = inp("xp", [D, T])
    tp = inp("tp", [1, T])
    invf_h = inp("invf", [1, 128])
    wq_h = inp("wq", [D, D])
    wk_h = inp("wk", [D, D])
    wv_h = inp("wv", [D, D])
    wo_h = inp("wo", [D, D])
    wqkv_h = inp("wqkv", [D, 3072])
    wob_h = inp("wob", [D, D])
    wg_h = inp("wg", [D, FF])
    wvl_h = inp("wvl", [D, FF])
    wout_h = inp("wout", [FF, D])
    bg_h = inp("bg", [128, FC])
    bv_h = inp("bv", [128, FC])
    bo_h = inp("bo", [128, DC])
    idf_h = inp("idf", [128, 128])
    jm_h = inp("jm", [128, 128])
    ohs_h = inp("ohs", [128, H * 16])
    ohr_h = inp("ohr", [16, H * 128])
    y_h = nc.dram_tensor("y", [D, T], F32, kind="ExternalOutput").ap()

    gq_d = nc.dram_tensor("gq_d", [D, FF], F16, kind="Internal").ap()
    vq_d = nc.dram_tensor("vq_d", [D, FF], F16, kind="Internal").ap()
    oq_d = nc.dram_tensor("oq_d", [FF, D], F16, kind="Internal").ap()
    r2_d = nc.dram_tensor("r2_d", [D, T], F32, kind="Internal").ap()
    r3_d = nc.dram_tensor("r3_d", [D, T], F32, kind="Internal").ap()
    r4_d = nc.dram_tensor("r4_d", [D, T], F32, kind="Internal").ap()
    g3_d = nc.dram_tensor("g3_d", [128, 16], F32, kind="Internal").ap()
    if MK_DEBUG:
        dbg = {n: nc.dram_tensor("o_" + n, shp, dt, kind="ExternalOutput").ap()
               for n, shp, dt in (("r2", [D, T], F32), ("r3", [D, T], F32),
                                  ("r4", [D, T], F32), ("gq", [128, FF], F16),
                                  ("scl", [1, 32], F32), ("sclc", [128, 8], F32),
                                  ("xq", [128, DC * T], F16), ("gv", [128, 8 * 512], F16),
                                  ("rows", [12, T], F32), ("sbc", [128, 512], F32),
                                  ("oq", [128, D], F16),
                                  ("qb", [128, H * 512], BF), ("kb", [128, H * 512], BF),
                                  ("vb", [128, H * 512], BF), ("ob", [128, H * 512], BF))}

    stack = ExitStack()
    kb = KB(nc, stack)

    def sb(name, shape, dtype, st=stack):
        return st.enter_context(nc.sbuf_tensor(name, shape, dtype))

    invf_sb = sb("invf_sb", [1, 128], F32)
    sgn = sb("sgn", [128, 1], F32)
    jmat = sb("jmat", [128, 128], BF)
    ones_bf = sb("ones_bf", [128, 1], BF)
    ones_f = sb("ones_f", [1, 128], F32)
    ones_fc = sb("ones_fc", [128, 1], F32)
    eb30 = sb("eb30", [128, 1], F32)
    eps1 = sb("eps1", [1, 1], F32)
    bg_sb = sb("bg_sb", [128, FC], F32)
    bv_sb = sb("bv_sb", [128, FC], F32)
    bo_sb = sb("bo_sb", [128, DC], F32)
    idf = sb("idf_sb", [128, 128], F32)
    rowsT = [sb(f"rowsT{i}", [1, T], F32) for i in range(12)]
    scl = sb("scl", [1, 32], F32)
    sclc = sb("sclc", [128, 8], F32)
    mbc = sb("mbc", [128, T], F32)
    ubc = sb("ubc", [128, T], F32)
    wbc = sb("wbc", [128, T], F32)
    sbc = sb("sbc", [128, 512], F32)
    hi_sb = sb("hi_sb", [128, T], F32)
    lo_sb = sb("lo_sb", [128, T], F32)
    g3_sb = sb("g3_sb", [128, 16], F32)

    def row(i):
        return rowsT[i][0:1, :]
    grow = row(3)

    PS = [stack.enter_context(nc.psum_tensor(f"PS{i}", [128, T], F32)) for i in range(4)]
    PSA, PSB, PSC, PSD = PS

    kb.op("vector", lambda e: e.memset(ones_bf[:, :], 1.0))
    kb.op("vector", lambda e: e.memset(ones_f[:, :], 1.0))
    kb.op("vector", lambda e: e.memset(ones_fc[:, :], 1.0))
    kb.op("vector", lambda e: e.memset(eb30[:, :], EXPB))
    kb.op("vector", lambda e: e.memset(sgn[0:64, 0:1], -1.0))
    kb.op("vector", lambda e: e.memset(sgn[64:128, 0:1], 1.0))
    kb.op("vector", lambda e: e.memset(eps1[:, :], EPS), inc="dv")
    for dst, src in ((invf_sb, invf_h), (bg_sb, bg_h), (bv_sb, bv_h),
                     (bo_sb, bo_h), (idf, idf_h)):
        kb.dma("sync", dst[:, :], src[:, :], "sy")
    sy_init = kb.ct["sy"]

    def stats_rows(src_bf, sq_slots, r_m, r_rs, r_b=None, f32=False, tw=T):
        ones_x = ones_fc if f32 else ones_bf
        nh = tw // 512
        pe_sq = {}
        for c in range(DC):
            sq = sq_slots[c % 2]
            if c >= 2:
                kb.wait("scalar", "pe", pe_sq[c - 2])
            kb.op("scalar", lambda e, c=c, sq=sq: e.activation(
                sq[:, 0:tw], src_bf[:, c * tw:(c + 1) * tw], AF.Square), inc="ac")
            kb.wait("tensor", "ac")
            for hf in range(nh):
                kb.op("tensor", lambda e, c=c, hf=hf: e.matmul(
                    PSC[0:1, hf * 512:(hf + 1) * 512], ones_x[:, :],
                    src_bf[:, c * tw + hf * 512: c * tw + (hf + 1) * 512],
                    start=(c == 0), stop=(c == DC - 1)),
                    inc="pe" if c == DC - 1 else None)
            for hf in range(nh):
                kb.op("tensor", lambda e, c=c, hf=hf, sq=sq: e.matmul(
                    PSD[0:1, hf * 512:(hf + 1) * 512], ones_bf[:, :],
                    sq[:, hf * 512:(hf + 1) * 512],
                    start=(c == 0), stop=True), inc="pe")
            pe_sq[c] = kb.ct["pe"]
        pe_stats = kb.ct["pe"]
        kb.wait("vector", "pe", pe_stats)
        kb.op("vector", lambda e: e.tensor_scalar(row(r_m)[0:1, 0:tw], PSC[0:1, 0:tw], 1.0 / D, None, ALU.mult, ALU.bypass))
        kb.op("vector", lambda e: e.tensor_scalar(row(11)[0:1, 0:tw], PSD[0:1, 0:tw], 1.0 / D, None, ALU.mult, ALU.bypass))
        kb.op("vector", lambda e: e.tensor_tensor(row(10)[0:1, 0:tw], row(r_m)[0:1, 0:tw], row(r_m)[0:1, 0:tw], ALU.mult))
        kb.op("vector", lambda e: e.tensor_tensor(row(11)[0:1, 0:tw], row(11)[0:1, 0:tw], row(10)[0:1, 0:tw], ALU.subtract), inc="dv")
        kb.wait("scalar", "dv")
        kb.op("scalar", lambda e: e.activation(row(11)[0:1, 0:tw], row(11)[0:1, 0:tw], AF.Ln, bias=eps1[0:1, :]))
        kb.op("scalar", lambda e: e.activation(row(r_rs)[0:1, 0:tw], row(11)[0:1, 0:tw], AF.Exp, scale=-0.5), inc="ac")
        if r_b is not None:
            kb.wait("vector", "ac")
            kb.op("vector", lambda e: e.tensor_tensor(row(r_b)[0:1, 0:tw], row(r_m)[0:1, 0:tw], row(r_rs)[0:1, 0:tw], ALU.mult), inc="dv")
        return kb.ct["pe"]

    def bcast(psum, slot, out_sb, width=T):
        row_ap = row(slot)
        base = 0
        kb.wait("tensor", "dv")
        kb.wait("tensor", "ac")
        nh = (width + 511) // 512
        for hf in range(nh):
            w = min(512, width - hf * 512)
            kb.op("tensor", lambda e, hf=hf, w=w, base=base, row_ap=row_ap: e.matmul(
                psum[:, hf * 512:hf * 512 + w], ones_f[:, :],
                row_ap[0:1, hf * 512:hf * 512 + w], start=True, stop=True), inc="pe")
        kb.wait("vector", "pe")
        kb.op("vector", lambda e: e.tensor_copy(out_sb[:, 0:width], psum[:, 0:width]), inc="dv")

    def bcast_col(scl_slot, col_slot):
        kb.wait("tensor", "dv")
        kb.op("tensor", lambda e: e.matmul(
            PSD[:, 0:1], ones_f[:, :], scl[0:1, scl_slot:scl_slot + 1],
            start=True, stop=True), inc="pe")
        kb.wait("vector", "pe")
        kb.op("vector", lambda e: e.tensor_copy(sclc[:, col_slot:col_slot + 1], PSD[:, 0:1]), inc="dv")

    T2 = 512
    seg1 = ExitStack()
    xbf = sb("xbf", [128, DC * T2], BF, st=seg1)
    obuf = xbf
    qbuf = sb("qbuf", [128, H * T2], BF, st=seg1)
    kbuf = sb("kbuf", [128, H * T2], BF, st=seg1)
    vbuf = sb("vbuf", [128, H * T2], BF, st=seg1)
    abc = sb("abc", [128, T2], F32, st=seg1)
    bbc = sb("bbc", [128, T2], F32, st=seg1)
    wbuf = [sb(f"wbuf{i}", [128, DC * 128], BF, st=seg1) for i in range(2)]
    sqs = [sb(f"sqs{i}", [128, T2], BF, st=seg1) for i in range(2)]
    cc = sb("cc", [128, T2], F32, st=seg1)
    ss = sb("ss", [128, T2], F32, st=seg1)
    ut = sb("ut", [128, 2 * T2], BF, st=seg1)
    vt = sb("vt", [128, 2 * T2], BF, st=seg1)
    an = sb("an", [128, T2], F32, st=seg1)
    oh_s = sb("oh_s", [128, H * 16], BF, st=seg1)
    oh_r = sb("oh_r", [16, H * 128], BF, st=seg1)
    ones16 = sb("ones16", [16, 1], BF, st=seg1)
    prod = [sb(f"prod{i}", [128, T2], BF, st=seg1) for i in range(2)]
    Ptile = sb("Ptile", [16, T2], BF, st=seg1)
    pb_bf = [sb(f"pb{i}", [128, T2], BF, st=seg1) for i in range(2)]
    avt = sb("avt", [128, T2], BF, st=seg1)
    gbuf = [sb(f"gbuf{i}", [128, 512], F32, st=seg1) for i in range(2)]
    gacc = sb("gacc", [128, 3 * 256], F32, st=seg1)
    gsum = sb("gsum", [128, 4], F32, st=seg1)
    qin = [sb(f"qin{i}", [128, 1024], F32, st=seg1) for i in range(2)]
    qout = [sb(f"qout{i}", [128, 1024], F16, st=seg1) for i in range(2)]
    rch = [sb(f"rch{i}", [128, T2], F32, st=seg1) for i in range(2)]
    tp_row = row(9)

    jld = kb.dma("gpsimd", jmat[:, :], jm_h[:, :], "gd")
    ohs_ld = kb.dma("gpsimd", oh_s[:, :], ohs_h[:, :], "gd")
    ohr_ld = kb.dma("gpsimd", oh_r[:, :], ohr_h[:, :], "gd")
    kb.op("vector", lambda e: e.memset(ones16[:, :], 1.0))
    kb.op("vector", lambda e: e.memset(gsum[:, 3:4], 0.0), inc="dv")

    gchunks = []
    for m_i, w_ap in enumerate((wg_h, wvl_h, wout_h)):
        nrow, ncol = w_ap.shape
        for rr in range(nrow // 128):
            for ci in range(ncol // 512):
                gchunks.append((w_ap[rr * 128:(rr + 1) * 128, ci * 512:(ci + 1) * 512], m_i))
    NG = len(gchunks)
    g_red = [None] * NG
    g_iter = iter(range(NG))

    def gamma_chunk(i):
        src, m_i = gchunks[i]
        slot = i % 2
        gs = "g0" if slot == 0 else "g1"
        if i >= 2:
            kb.wait("sync", "dv", g_red[i - 2])
        ld = kb.dma("sync", gbuf[slot][:, :], src, gs)
        kb.wait("vector", gs, ld)
        g_red[i] = kb.op("vector", lambda e, slot=slot, i=i: e.tensor_reduce(
            gacc[:, i:i + 1], gbuf[slot][:, :], AX.X, ALU.add, apply_absolute_value=True), inc="dv")

    qchunks = []
    for m_i, (w_ap, q_ap) in enumerate(((wg_h, gq_d), (wvl_h, vq_d), (wout_h, oq_d))):
        nrow, ncol = w_ap.shape
        for rr in range(nrow // 128):
            for ci in range(ncol // 1024):
                sl = (slice(rr * 128, (rr + 1) * 128), slice(ci * 1024, (ci + 1) * 1024))
                qchunks.append((w_ap[sl[0], sl[1]], q_ap[sl[0], sl[1]], m_i))
    NQ = len(qchunks)
    q_store = [None] * NQ
    q_done = [None] * NQ
    q_iter = iter(range(NQ))

    def quant_chunk(i):
        src, dst, m_i = qchunks[i]
        slot = i % 2
        qs = "q0" if slot == 0 else "q1"
        if i >= 2:
            kb.wait("sync", "dv", q_done[i - 2])
            kb.wait("vector", "sc", q_store[i - 2])
        ld = kb.dma("sync", qin[slot][:, :], src, qs)
        kb.wait("vector", qs, ld)
        kb.op("vector", lambda e, slot=slot, m_i=m_i: e.tensor_scalar(
            qin[slot][:, :], qin[slot][:, :], sclc[:, 4 + m_i:5 + m_i], MAGIC, ALU.mult, ALU.add))
        kb.op("vector", lambda e, slot=slot: e.tensor_scalar(
            qin[slot][:, :], qin[slot][:, :], MAGIC, 1.0, ALU.subtract, ALU.min))
        q_done[i] = kb.op("vector", lambda e, slot=slot: e.tensor_scalar(
            qout[slot][:, :], qin[slot][:, :], -1.0, None, ALU.max, ALU.bypass), inc="dv")
        kb.wait("scalar", "dv", q_done[i])
        q_store[i] = kb.dma("scalar", dst, qout[slot][:, :], "sc")

    kb.wait("gpsimd", "sy", sy_init)
    kb.wait("tensor", "sy", sy_init)
    kb.wait("tensor", "dv", 1)
    kb.wait("scalar", "sy", sy_init)

    w_used = {}
    pe_trig_prev = 0
    dv_out = {}

    def bg_work(n):
        if MK_NOBG:
            return
        for _ in range(n):
            nxt = next(g_iter, None)
            if nxt is not None:
                gamma_chunk(nxt)
                continue
            if not bg_work.quant_ok:
                return
            nxt = next(q_iter, None)
            if nxt is not None:
                quant_chunk(nxt)
    bg_work.quant_ok = False

    def bar(k):
        if MK_BAR & (1 << k):
            kb.barrier()

    for hf2 in range(2):
        tsl = slice(hf2 * T2, (hf2 + 1) * T2)
        bar(5)
        kb.wait("gpsimd", "pe")
        kb.wait("gpsimd", "ac")
        kb.wait("sync", "pe", pe_trig_prev)
        tld = kb.dma("sync", tp_row[0:1, 0:T2], tp[0:1, tsl], "t0")
        xld = kb.dma("gpsimd", xbf.rearrange("p (c t) -> p c t", c=DC),
                     xp[:, tsl].rearrange("(c p) t -> p c t", p=128), "gd")
        kb.wait("tensor", "gd", xld)
        kb.wait("tensor", "ac")
        kb.wait("tensor", "dv")
        kb.wait("scalar", "gd", xld)
        stats_rows(xbf, sqs, r_m=0, r_rs=1, r_b=2, tw=T2)
        dv_rows = kb.ct["dv"]
        bar(0)

        kb.wait("tensor", "dv", dv_rows)
        kb.wait("tensor", "t0", tld)
        kb.op("tensor", lambda e: e.matmul(
            PSD[:, 0:T2], invf_sb[:, :], tp_row[0:1, 0:T2], start=True, stop=True), inc="pe")
        pe_trig_prev = kb.ct["pe"]
        kb.wait("vector", "pe", pe_trig_prev)
        kb.op("vector", lambda e: e.tensor_scalar(an[:, :], PSD[:, 0:T2], INV2PI, None, ALU.mult, ALU.bypass))
        kb.op("vector", lambda e: e.tensor_scalar(cc[:, :], an[:, :], MAGIC, MAGIC, ALU.add, ALU.subtract))
        kb.op("vector", lambda e: e.tensor_tensor(ss[:, :], an[:, :], cc[:, :], ALU.subtract), inc="dv")
        kb.wait("scalar", "dv")
        kb.op("scalar", lambda e: e.activation(ss[:, :], ss[:, :], AF.Sin, scale=TWOPI), inc="ac")
        kb.op("vector", lambda e: e.tensor_scalar(an[:, :], an[:, :], 0.25, None, ALU.add, ALU.bypass))
        kb.op("vector", lambda e: e.tensor_scalar(cc[:, :], an[:, :], MAGIC, MAGIC, ALU.add, ALU.subtract))
        kb.op("vector", lambda e: e.tensor_tensor(cc[:, :], an[:, :], cc[:, :], ALU.subtract), inc="dv")
        kb.wait("scalar", "dv")
        kb.op("scalar", lambda e: e.activation(cc[:, :], cc[:, :], AF.Sin, scale=TWOPI), inc="ac")
        kb.wait("vector", "ac")

        bcast(PSA, 1, abc, width=T2)
        bcast(PSB, 2, bbc, width=T2)
        dv_bc = kb.ct["dv"]
        kb.wait("vector", "dv", dv_bc)
        for c in range(DC):
            kb.op("vector", lambda e, c=c: e.tensor_tensor(
                xbf[:, c * T2:(c + 1) * T2], xbf[:, c * T2:(c + 1) * T2], abc[:, :], ALU.mult))
            kb.op("vector", lambda e, c=c: e.tensor_tensor(
                xbf[:, c * T2:(c + 1) * T2], xbf[:, c * T2:(c + 1) * T2], bbc[:, :], ALU.subtract), inc="dv")
        dv_norm = kb.ct["dv"]
        bar(0)

        prev_uv = 0
        prev_comb = 0
        for pname, w_h_ap, dstbuf in (("k", wk_h, kbuf), ("q", wq_h, qbuf), ("v", wv_h, vbuf)):
            uv_done = {}
            vs_done = {}
            comb_done = {}
            evac = {}

            def emit_vswap(a, prev_comb=prev_comb):
                vs = PS[2 + a % 2]
                kb.wait("tensor", "dv", uv_done[a])
                cd = comb_done.get(a - (1 if MK_DEPTH1 else 2))
                kb.wait("tensor", "dv", cd if cd is not None else prev_comb)
                kb.op("tensor", lambda e, vs=vs, a=a: e.matmul(
                    vs[:, 0:T2], jmat[:, :], vt[:, (a % 2) * T2:((a % 2) + 1) * T2],
                    start=True, stop=True), inc="pe")
                vs_done[a] = kb.ct["pe"]

            def emit_combine(a, dstbuf):
                vs = PS[2 + a % 2]
                kb.wait("vector", "pe", vs_done[a])
                dst = dstbuf[:, a * T2:(a + 1) * T2]
                comb_done[a] = kb.op("vector", lambda e, vs=vs, dst=dst, a=a: e.scalar_tensor_tensor(
                    dst, vs[:, 0:T2], sgn[:, 0:1], ut[:, (a % 2) * T2:((a % 2) + 1) * T2],
                    ALU.mult, ALU.add), inc="dv")

            for a in range(16):
                slot = a % 2
                prev = w_used.get(slot)
                if prev is not None:
                    kb.wait("gpsimd", "pe", prev)
                wl = kb.dma("gpsimd", wbuf[slot].rearrange("p (c h) -> p c h", c=DC),
                            w_h_ap[:, a * 128:(a + 1) * 128].rearrange("(c p) h -> p c h", p=128), "gd")
                ps = PS[a % 2]
                kb.wait("tensor", "gd", wl)
                if a == 0:
                    kb.wait("tensor", "dv", dv_norm)
                    kb.wait("tensor", "gd", ohr_ld)
                dd = 1 if MK_DEPTH1 else 2
                if pname == "v":
                    ev = evac.get(a - dd)
                    if ev is not None:
                        kb.wait("tensor", "ac", ev)
                    else:
                        kb.wait("tensor", "dv", prev_uv)
                else:
                    rd = uv_done.get(a - dd)
                    kb.wait("tensor", "dv", rd if rd is not None else prev_uv)
                bar(9)
                for c in range(DC):
                    kb.op("tensor", lambda e, ps=ps, c=c, slot=slot: e.matmul(
                        ps[:, 0:T2], wbuf[slot][:, c * 128:(c + 1) * 128],
                        xbf[:, c * T2:(c + 1) * T2],
                        start=(c == 0), stop=(c == DC - 1)),
                        inc="pe" if c == DC - 1 else None)
                pe_done = kb.ct["pe"]
                w_used[slot] = pe_done
                bar(10)
                if pname == "v":
                    kb.wait("scalar", "pe", pe_done)
                    evac[a] = kb.op("scalar", lambda e, ps=ps, a=a: e.activation(
                        vbuf[:, a * T2:(a + 1) * T2], ps[:, 0:T2], AF.Identity), inc="ac")
                else:
                    kb.wait("vector", "pe", pe_done)
                    kb.op("vector", lambda e, ps=ps, a=a: e.tensor_tensor(
                        ut[:, (a % 2) * T2:((a % 2) + 1) * T2], ps[:, 0:T2], cc[:, :], ALU.mult))
                    uv_done[a] = kb.op("vector", lambda e, ps=ps, a=a: e.tensor_tensor(
                        vt[:, (a % 2) * T2:((a % 2) + 1) * T2], ps[:, 0:T2], ss[:, :], ALU.mult), inc="dv")
                    bar(11)
                    if a >= 1:
                        emit_vswap(a - 1)
                        emit_combine(a - 1, dstbuf)
                bar(6)
                bg_work(5)
            if pname != "v":
                emit_vswap(15)
                emit_combine(15, dstbuf)
                prev_uv = uv_done[15]
                prev_comb = comb_done[15]
            bar(1)

        kb.wait("tensor", "dv")
        kb.wait("tensor", "ac")
        kb.wait("gpsimd", "dv")
        if MK_DEBUG and hf2 == 0:
            kb.wait("sync", "dv")
            kb.wait("sync", "ac")
            kb.dma("sync", dbg["qb"][:, :], qbuf[:, :], "sy")
            kb.dma("sync", dbg["kb"][:, :], kbuf[:, :], "sy")
            kb.dma("sync", dbg["vb"][:, :], vbuf[:, :], "sy")
            for eng in ("vector", "scalar", "gpsimd", "tensor"):
                kb.wait(eng, "sy")
        exp_done = 0
        pq_done = 0
        for qh in range(H):
            for kh in range(H):
                pslot = kh % 2
                eng = "vector" if kh % 2 == 0 else "gpsimd"
                sem = "dv" if kh % 2 == 0 else "po"
                if qh > 0 or kh >= 2:
                    kb.wait(eng, "pe")
                pm = kb.op(eng, lambda e, qh=qh, kh=kh, pslot=pslot: e.tensor_tensor(
                    prod[pslot][:, :], qbuf[:, qh * T2:(qh + 1) * T2],
                    kbuf[:, kh * T2:(kh + 1) * T2], ALU.mult), inc=sem)
                kb.wait("tensor", sem, pm)
                if kh == 0:
                    kb.wait("tensor", "ac", exp_done)
                kb.op("tensor", lambda e, kh=kh, pslot=pslot: e.matmul(
                    PSC[0:16, 0:T2], oh_s[:, kh * 16:(kh + 1) * 16],
                    prod[pslot][:, :], start=(kh == 0), stop=True), inc="pe")
                bar(7)
            pe_sc = kb.ct["pe"]
            kb.wait("scalar", "pe", pe_sc)
            kb.wait("scalar", "dv", pq_done)
            exp_done = kb.op("scalar", lambda e: e.activation(
                Ptile[:, :], PSC[0:16, 0:T2], AF.Exp, bias=eb30[0:16, 0:1], scale=SSCALE), inc="ac")
            kb.wait("tensor", "ac", exp_done)
            kb.op("tensor", lambda e: e.matmul(
                PSD[0:1, 0:T2], ones16[:, :], Ptile[:, :], start=True, stop=True), inc="pe")
            pe_l = kb.ct["pe"]
            kb.wait("vector", "pe", pe_l)
            kb.op("vector", lambda e: e.tensor_copy(row(3)[0:1, 0:T2], PSD[0:1, 0:T2]))
            kb.op("vector", lambda e: e.reciprocal(row(3)[0:1, 0:T2], row(3)[0:1, 0:T2]), inc="dv")
            dv_rinv = kb.ct["dv"]
            kb.wait("tensor", "dv", dv_rinv)
            kb.op("tensor", lambda e: e.matmul(
                PSB[:, 0:T2], ones_f[:, :], row(3)[0:1, 0:T2], start=True, stop=True), inc="pe")
            pe_rb = kb.ct["pe"]
            mac_done = 0
            pbe_hist = {}
            for kh in range(H):
                pbslot = kh % 2
                if kh >= 2:
                    kb.wait("tensor", "ac", pbe_hist[kh - 2])
                kb.op("tensor", lambda e, kh=kh, pbslot=pbslot: e.matmul(
                    PSA[:, pbslot * T2:(pbslot + 1) * T2], oh_r[:, kh * 128:(kh + 1) * 128],
                    Ptile[:, :], start=True, stop=True), inc="pe")
                pe_pb = kb.ct["pe"]
                kb.wait("scalar", "pe", pe_pb)
                if kh >= 2:
                    kb.wait("scalar", "dv", mac_done)
                pbe = kb.op("scalar", lambda e, pbslot=pbslot: e.activation(
                    pb_bf[pbslot][:, :], PSA[:, pbslot * T2:(pbslot + 1) * T2], AF.Identity), inc="ac")
                pbe_hist[kh] = pbe
                kb.wait("vector", "ac", pbe)
                if kh == 0:
                    kb.op("vector", lambda e, qh=qh: e.tensor_tensor(
                        obuf[:, qh * T2:(qh + 1) * T2], pb_bf[0][:, :],
                        vbuf[:, 0:T2], ALU.mult), inc="dv")
                else:
                    kb.op("vector", lambda e, kh=kh, pbslot=pbslot: e.tensor_tensor(
                        avt[:, :], pb_bf[pbslot][:, :], vbuf[:, kh * T2:(kh + 1) * T2], ALU.mult))
                    kb.op("vector", lambda e, qh=qh: e.tensor_tensor(
                        obuf[:, qh * T2:(qh + 1) * T2], obuf[:, qh * T2:(qh + 1) * T2],
                        avt[:, :], ALU.add), inc="dv")
                mac_done = kb.ct["dv"]
                bar(8)
            kb.wait("vector", "pe", pe_rb)
            pq_done = kb.op("vector", lambda e, qh=qh: e.tensor_tensor(
                obuf[:, qh * T2:(qh + 1) * T2], obuf[:, qh * T2:(qh + 1) * T2],
                PSB[:, 0:T2], ALU.mult), inc="dv")
            bar(2)
            bg_work(4)

        if MK_DEBUG and hf2 == 0:
            kb.wait("sync", "dv")
            kb.dma("sync", dbg["ob"][:, :], obuf[:, :], "sy")
            for eng in ("vector", "scalar", "gpsimd", "tensor"):
                kb.wait(eng, "sy")

        kb.wait("gpsimd", "pe")
        w_used2 = {}
        st_out = {}
        for o in range(DC):
            slot = o % 2
            rs = "r0" if slot == 0 else "r1"
            ssm = "s0" if slot == 0 else "s1"
            prev = w_used2.get(slot)
            if prev is not None:
                kb.wait("gpsimd", "pe", prev)
            wl = kb.dma("gpsimd", wbuf[slot].rearrange("p (c h) -> p c h", c=DC),
                        wo_h[:, o * 128:(o + 1) * 128].rearrange("(c p) h -> p c h", p=128), "gd")
            d_prev = dv_out.get((hf2, o - 2))
            if d_prev is not None:
                kb.wait("sync", "dv", d_prev)
                kb.wait("sync", ssm, st_out[(hf2, o - 2)])
            else:
                p_st = st_out.get((hf2 - 1, 14 + o))
                if p_st is not None:
                    kb.wait("sync", ssm, p_st)
            rl = kb.dma("sync", rch[slot][:, :], xp[o * 128:(o + 1) * 128, tsl], rs)
            ps = PS[2 + o % 2]
            kb.wait("tensor", "gd", wl)
            if o < 2:
                kb.wait("tensor", "dv")
                kb.wait("tensor", "ac")
            else:
                kb.wait("tensor", "dv", dv_out[(hf2, o - 2)])
            for c in range(DC):
                kb.op("tensor", lambda e, ps=ps, c=c, slot=slot: e.matmul(
                    ps[:, 0:T2], wbuf[slot][:, c * 128:(c + 1) * 128],
                    obuf[:, c * T2:(c + 1) * T2],
                    start=(c == 0), stop=(c == DC - 1)),
                    inc="pe" if c == DC - 1 else None)
            w_used2[slot] = kb.ct["pe"]
            kb.wait("vector", "pe", w_used2[slot])
            kb.wait("vector", rs, rl)
            dv_out[(hf2, o)] = kb.op("vector", lambda e, ps=ps, slot=slot: e.tensor_tensor(
                rch[slot][:, :], ps[:, 0:T2], rch[slot][:, :], ALU.add), inc="dv")
            kb.wait("sync", "dv", dv_out[(hf2, o)])
            st_out[(hf2, o)] = kb.dma("sync", r2_d[o * 128:(o + 1) * 128, tsl], rch[slot][:, :], ssm)
            bar(3)
            bg_work(2)

        if hf2 == 0:
            bar(4)
            while True:
                nxt = next(g_iter, None)
                if nxt is None:
                    break
                gamma_chunk(nxt)
            for m_i in range(3):
                kb.op("vector", lambda e, m_i=m_i: e.tensor_reduce(
                    gsum[:, m_i:m_i + 1], gacc[:, m_i * 256:(m_i + 1) * 256], AX.X, ALU.add), inc="dv")
            kb.wait("tensor", "dv")
            kb.wait("tensor", "ac")
            kb.op("tensor", lambda e: e.matmul(
                PSD[0:1, 0:4], ones_fc[:, :], gsum[:, 0:4], start=True, stop=True), inc="pe")
            kb.wait("vector", "pe")
            NW = (float(D * FF), float(D * FF), float(FF * D))
            for m_i in range(3):
                kb.op("vector", lambda e, m_i=m_i: e.tensor_scalar(
                    scl[0:1, m_i:m_i + 1], PSD[0:1, m_i:m_i + 1], 1.0 / NW[m_i], 1e-5,
                    ALU.mult, ALU.max), inc="dv")
            kb.op("vector", lambda e: e.memset(scl[0:1, 3:4], 1.0))
            kb.op("vector", lambda e: e.reciprocal(scl[0:1, 4:8], scl[0:1, 0:4]), inc="dv")
            for m_i in range(3):
                bcast_col(m_i, m_i)
                bcast_col(4 + m_i, 4 + m_i)
            bar(4)
            bg_work.quant_ok = True

    while True:
        nxt = next(q_iter, None)
        if nxt is None:
            break
        quant_chunk(nxt)

    kb.barrier()
    seg1.close()

    def ffn_phase(r_src, r_dst):
        uid = ffn_phase.uid = getattr(ffn_phase, "uid", 0) + 1
        segA = ExitStack()
        x_q = sb(f"x_q{uid}", [128, DC * T], F16, st=segA)

        segB = ExitStack()
        r_sb = sb(f"rsb{uid}", [128, DC * T], F32, st=segB)
        ctmp = sb(f"ctmp{uid}", [128, T], F32, st=segB)
        sqs2 = [sb(f"sq2_{uid}_{i}", [128, T], BF, st=segB) for i in range(2)]

        rld = kb.dma("sync", r_sb.rearrange("p (c t) -> p c t", c=DC),
                     r_src.rearrange("(c p) t -> p c t", p=128), "sy")
        kb.wait("scalar", "sy", rld)
        kb.wait("tensor", "sy", rld)
        stats_rows(r_sb, sqs2, r_m=0, r_rs=1, f32=True)
        bcast(PSA, 0, mbc)
        dv_m = kb.ct["dv"]
        kb.wait("vector", "dv", dv_m)
        kb.op("vector", lambda e: e.memset(hi_sb[:, :], -3.0e38))
        kb.op("vector", lambda e: e.memset(lo_sb[:, :], 3.0e38))
        for c in range(DC):
            kb.op("vector", lambda e, c=c: e.tensor_tensor(
                ctmp[:, :], r_sb[:, c * T:(c + 1) * T], mbc[:, :], ALU.subtract))
            kb.op("vector", lambda e: e.tensor_tensor(hi_sb[:, :], hi_sb[:, :], ctmp[:, :], ALU.max))
            kb.op("vector", lambda e: e.tensor_tensor(lo_sb[:, :], lo_sb[:, :], ctmp[:, :], ALU.min), inc="dv")
        kb.op("vector", lambda e: e.tensor_scalar(lo_sb[:, :], lo_sb[:, :], -1.0, None, ALU.mult, ALU.bypass))
        kb.op("vector", lambda e: e.tensor_tensor(hi_sb[:, :], hi_sb[:, :], lo_sb[:, :], ALU.max), inc="dv")
        kb.wait("tensor", "dv")
        for j in range(8):
            kb.op("tensor", lambda e, j=j: e.transpose(
                PSB[:, j * 128:(j + 1) * 128], hi_sb[:, j * 128:(j + 1) * 128], idf[:, :]), inc="pe")
        kb.wait("vector", "pe")
        kb.op("vector", lambda e: e.tensor_reduce(
            g3_sb[:, 0:8], PSB.rearrange("p (j v) -> p j v", j=8), AX.X, ALU.max), inc="dv")
        kb.wait("sync", "dv")
        kb.dma("sync", g3_d[:, 0:8], g3_sb[:, 0:8], "sy")
        kb.wait("sync", "sy")
        grld = kb.dma("sync", grow.rearrange("o (j p) -> o j p", j=8),
                      g3_d[:, 0:8].rearrange("p j -> () j p"), "sy",
                      allow_slow_non_contiguous=True)
        kb.wait("vector", "sy", grld)
        kb.op("vector", lambda e: e.tensor_tensor(row(4), grow[0:1, :], row(1), ALU.mult))
        kb.op("vector", lambda e: e.tensor_scalar(row(4), row(4), 1e-5, None, ALU.max, ALU.bypass), inc="dv")
        kb.op("vector", lambda e: e.reciprocal(row(6), row(4)))
        kb.op("vector", lambda e: e.tensor_tensor(row(6), row(6), row(1), ALU.mult))
        kb.op("vector", lambda e: e.tensor_scalar(row(6), row(6), 127.0, None, ALU.mult, ALU.bypass))
        kb.op("vector", lambda e: e.tensor_scalar(row(5), row(4), 1.0 / 127.0, None, ALU.mult, ALU.bypass), inc="dv")
        bcast(PSA, 6, ubc)
        bcast(PSB, 5, wbc)
        dv_uw = kb.ct["dv"]
        kb.wait("vector", "dv", dv_uw)
        for c in range(DC):
            kb.op("vector", lambda e, c=c: e.tensor_tensor(
                ctmp[:, :], r_sb[:, c * T:(c + 1) * T], mbc[:, :], ALU.subtract))
            kb.op("vector", lambda e: e.tensor_tensor(ctmp[:, :], ctmp[:, :], ubc[:, :], ALU.mult))
            kb.op("vector", lambda e: e.tensor_scalar(ctmp[:, :], ctmp[:, :], MAGIC, MAGIC, ALU.add, ALU.subtract))
            kb.op("vector", lambda e, c=c: e.tensor_tensor(
                x_q[:, c * T:(c + 1) * T], ctmp[:, :], wbc[:, :], ALU.mult), inc="dv")
        if MK_DEBUG and uid == 1:
            kb.wait("sync", "dv")
            kb.dma("sync", dbg["xq"][:, :], x_q[:, :], "sy")
        kb.barrier()
        segB.close()

        segC = ExitStack()
        gvq = sb(f"gvq{uid}", [128, FC * 512], F16, st=segC)
        wgq = [sb(f"wgq{uid}_{i}", [128, 2048], F16, st=segC) for i in range(2)]
        wvq = [sb(f"wvq{uid}_{i}", [128, 2048], F16, st=segC) for i in range(2)]
        woq = [sb(f"woq{uid}_{i}", [128, FF], F16, st=segC) for i in range(1)]
        gtmp = sb(f"gtmp{uid}", [128, 512], F16, st=segC)
        vtmp = sb(f"vtmp{uid}", [128, 512], F16, st=segC)
        rc2 = [sb(f"rc2{uid}_{i}", [128, 512], F32, st=segC) for i in range(2)]

        st2 = {}
        for hf in range(2):
            kb.wait("gpsimd", "pe")
            wg_used = {}
            gv_done = {}
            for j in range(FC):
                slot = j % 2
                fs = "f0" if slot == 0 else "f1"
                prev = wg_used.get(slot)
                if prev is not None:
                    kb.wait("gpsimd", "pe", prev)
                kb.dma("gpsimd", wgq[slot].rearrange("p (c h) -> p c h", c=DC),
                       gq_d[:, j * 128:(j + 1) * 128].rearrange("(c p) h -> p c h", p=128), fs)
                vl = kb.dma("gpsimd", wvq[slot].rearrange("p (c h) -> p c h", c=DC),
                            vq_d[:, j * 128:(j + 1) * 128].rearrange("(c p) h -> p c h", p=128), fs)
                ps = PS[j % 2]
                kb.wait("tensor", fs, vl)
                if j < 2:
                    kb.wait("tensor", "dv")
                else:
                    kb.wait("tensor", "dv", gv_done[j - 2])
                for c in range(DC):
                    kb.op("tensor", lambda e, ps=ps, c=c, slot=slot, hf=hf: e.matmul(
                        ps[:, 0:512], wgq[slot][:, c * 128:(c + 1) * 128],
                        x_q[:, c * T + hf * 512:c * T + (hf + 1) * 512],
                        start=(c == 0), stop=(c == DC - 1)),
                        inc="pe" if c == DC - 1 else None)
                for c in range(DC):
                    kb.op("tensor", lambda e, ps=ps, c=c, slot=slot, hf=hf: e.matmul(
                        ps[:, 512:1024], wvq[slot][:, c * 128:(c + 1) * 128],
                        x_q[:, c * T + hf * 512:c * T + (hf + 1) * 512],
                        start=(c == 0), stop=(c == DC - 1)),
                        inc="pe" if c == DC - 1 else None)
                pe_gv = kb.ct["pe"]
                wg_used[slot] = pe_gv
                kb.wait("scalar", "pe", pe_gv)
                if j >= 1:
                    kb.wait("scalar", "dv", gv_done.get(j - 1, 0))
                kb.op("scalar", lambda e, ps=ps, j=j: e.activation(
                    gtmp[:, :], ps[:, 0:512], AF.Silu, bias=bg_sb[:, j:j + 1], scale=sclc[:, 0:1]), inc="ac")
                kb.op("scalar", lambda e, ps=ps, j=j: e.activation(
                    vtmp[:, :], ps[:, 512:1024], AF.Identity, bias=bv_sb[:, j:j + 1], scale=sclc[:, 1:2]), inc="ac")
                kb.wait("vector", "ac")
                gv_done[j] = kb.op("vector", lambda e, j=j: e.tensor_tensor(
                    gvq[:, j * 512:(j + 1) * 512], gtmp[:, :], vtmp[:, :], ALU.mult), inc="dv")
            kb.op("vector", lambda e: e.tensor_reduce(
                hi_sb[:, 0:512], gvq.rearrange("p (j v) -> p v j", j=FC), AX.X, ALU.max,
                apply_absolute_value=True), inc="dv")
            kb.wait("tensor", "dv")
            for j in range(4):
                kb.op("tensor", lambda e, j=j: e.transpose(
                    PSB[:, j * 128:(j + 1) * 128], hi_sb[:, j * 128:(j + 1) * 128], idf[:, :]), inc="pe")
            kb.wait("vector", "pe")
            kb.op("vector", lambda e: e.tensor_reduce(
                g3_sb[:, 0:4], PSB[:, 0:512].rearrange("p (j v) -> p j v", j=4), AX.X, ALU.max), inc="dv")
            kb.wait("sync", "dv")
            kb.dma("sync", g3_d[:, 8:12], g3_sb[:, 0:4], "sy")
            kb.wait("sync", "sy")
            grld2 = kb.dma("sync", grow[0:1, 0:512].rearrange("o (j p) -> o j p", j=4),
                           g3_d[:, 8:12].rearrange("p j -> () j p"), "sy",
                           allow_slow_non_contiguous=True)
            kb.wait("vector", "sy", grld2)
            kb.op("vector", lambda e: e.tensor_scalar(
                row(7)[0:1, 0:512], grow[0:1, 0:512], 1e-5, None, ALU.max, ALU.bypass), inc="dv")
            kb.op("vector", lambda e: e.reciprocal(row(8)[0:1, 0:512], row(7)[0:1, 0:512]))
            kb.op("vector", lambda e: e.tensor_scalar(
                row(8)[0:1, 0:512], row(8)[0:1, 0:512], 127.0, None, ALU.mult, ALU.bypass))
            kb.op("vector", lambda e: e.tensor_scalar(
                row(9)[0:1, 0:512], row(7)[0:1, 0:512], scl[0:1, 2:3], 1.0 / 127.0,
                ALU.mult, ALU.mult), inc="dv")
            bcast(PSA, 8, ubc, width=512)
            bcast(PSB, 9, sbc, width=512)
            dv_b2 = kb.ct["dv"]
            kb.wait("vector", "dv", dv_b2)
            qz_done = {}
            for j in range(FC):
                kb.op("vector", lambda e, j=j: e.tensor_tensor(
                    gvq[:, j * 512:(j + 1) * 512], gvq[:, j * 512:(j + 1) * 512], ubc[:, 0:512], ALU.mult))
                qz_done[j] = kb.op("vector", lambda e, j=j: e.tensor_scalar(
                    gvq[:, j * 512:(j + 1) * 512], gvq[:, j * 512:(j + 1) * 512],
                    MAGIC, MAGIC, ALU.add, ALU.subtract), inc="dv")
            dv_q = kb.ct["dv"]
            if MK_DEBUG and uid == 1 and hf == 0:
                kb.wait("sync", "dv")
                kb.dma("sync", dbg["gv"][:, :], gvq[:, 0:8 * 512], "sy")
                kb.dma("sync", dbg["sbc"][:, :], sbc[:, 0:512], "sy")
                for ri in range(12):
                    kb.dma("sync", dbg["rows"][ri:ri + 1, :], rowsT[ri][0:1, :], "sy")
                kb.dma("sync", dbg["oq"][:, :], oq_d[0:128, :], "sy")
            kb.wait("gpsimd", "pe")
            wo_used = {}
            epi = {}
            for o in range(DC):
                slot = o % 2
                rs = "r0" if slot == 0 else "r1"
                ssm = "s0" if slot == 0 else "s1"
                prev = wo_used.get(0)
                if prev is not None:
                    kb.wait("gpsimd", "pe", prev)
                ol = kb.dma("gpsimd", woq[0].rearrange("p (c h) -> p c h", c=FC),
                            oq_d[:, o * 128:(o + 1) * 128].rearrange("(c p) h -> p c h", p=128), "w0")
                if o >= 2:
                    kb.wait("sync", "dv", epi[o - 2])
                    kb.wait("sync", ssm, st2[(hf, o - 2)])
                else:
                    p_st = st2.get((hf - 1, 14 + o))
                    if p_st is not None:
                        kb.wait("sync", ssm, p_st)
                rl = kb.dma("sync", rc2[slot][:, :],
                            r_src[o * 128:(o + 1) * 128, hf * 512:(hf + 1) * 512], rs)
                ps = PS[2 + (o % 2)]
                kb.wait("tensor", "w0", ol)
                if o < 2:
                    kb.wait("tensor", "dv", dv_q)
                else:
                    kb.wait("tensor", "dv", epi[o - 2])
                for c in range(FC):
                    kb.op("tensor", lambda e, ps=ps, c=c: e.matmul(
                        ps[:, 0:512], woq[0][:, c * 128:(c + 1) * 128],
                        gvq[:, c * 512:(c + 1) * 512],
                        start=(c == 0), stop=(c == FC - 1)),
                        inc="pe" if c == FC - 1 else None)
                wo_used[0] = kb.ct["pe"]
                kb.wait("vector", "pe", wo_used[0])
                kb.wait("vector", rs, rl)
                kb.op("vector", lambda e, ps=ps: e.tensor_tensor(
                    lo_sb[:, 0:512], ps[:, 0:512], sbc[:, 0:512], ALU.mult))
                epi[o] = kb.op("vector", lambda e, o=o, slot=slot: e.scalar_tensor_tensor(
                    rc2[slot][:, :], lo_sb[:, 0:512], bo_sb[:, o:o + 1], rc2[slot][:, :],
                    ALU.add, ALU.add), inc="dv")
                kb.wait("sync", "dv", epi[o])
                st2[(hf, o)] = kb.dma("sync", r_dst[o * 128:(o + 1) * 128, hf * 512:(hf + 1) * 512], rc2[slot][:, :], ssm)
        kb.barrier()
        segC.close()
        segA.close()

    ffn_phase(r2_d, r3_d)

    seg4 = ExitStack()
    wbuf4 = [sb(f"wbuf4_{i}", [128, DC * 128], BF, st=seg4) for i in range(2)]
    r_sb4 = sb("rsb4", [128, DC * T], F32, st=seg4)
    xn4 = sb("xn4", [128, DC * T], BF, st=seg4)
    vcat = sb("vcat", [128, 4 * T], BF, st=seg4)
    sqs4 = [sb(f"sq4_{i}", [128, T], BF, st=seg4) for i in range(2)]

    rld4 = kb.dma("sync", r_sb4.rearrange("p (c t) -> p c t", c=DC),
                  r3_d.rearrange("(c p) t -> p c t", p=128), "sy")
    kb.wait("scalar", "sy", rld4)
    kb.op("scalar", lambda e: e.activation(xn4[:, :], r_sb4[:, :], AF.Identity), inc="ac")
    kb.wait("tensor", "ac")
    stats_rows(xn4, sqs4, r_m=0, r_rs=1, r_b=2)
    bcast(PSA, 1, mbc)
    bcast(PSB, 2, ubc)
    dv_bc4 = kb.ct["dv"]
    kb.wait("vector", "dv", dv_bc4)
    for c in range(DC):
        kb.op("vector", lambda e, c=c: e.tensor_tensor(
            xn4[:, c * T:(c + 1) * T], xn4[:, c * T:(c + 1) * T], mbc[:, :], ALU.mult))
        kb.op("vector", lambda e, c=c: e.tensor_tensor(
            xn4[:, c * T:(c + 1) * T], xn4[:, c * T:(c + 1) * T], ubc[:, :], ALU.subtract), inc="dv")
    dv_n4 = kb.ct["dv"]

    wb_used = {}
    evac4 = {}
    for a in range(24):
        slot = a % 2
        prev = wb_used.get(slot)
        if prev is not None:
            kb.wait("gpsimd", "pe", prev)
        wl = kb.dma("gpsimd", wbuf4[slot].rearrange("p (c h) -> p c h", c=DC),
                    wqkv_h[:, a * 128:(a + 1) * 128].rearrange("(c p) h -> p c h", p=128), "gd")
        ps = PS[a % 2]
        kb.wait("tensor", "gd", wl)
        if a == 0:
            kb.wait("tensor", "dv", dv_n4)
        ev = evac4.get(a - 2)
        if ev is not None:
            kb.wait("tensor", "ac", ev)
        for hfq in range(2):
            for c in range(DC):
                kb.op("tensor", lambda e, ps=ps, hfq=hfq, c=c, slot=slot: e.matmul(
                    ps[:, hfq * 512:(hfq + 1) * 512],
                    wbuf4[slot][:, c * 128:(c + 1) * 128],
                    xn4[:, c * T + hfq * 512:c * T + (hfq + 1) * 512],
                    start=(c == 0), stop=(c == DC - 1)),
                    inc="pe" if c == DC - 1 else None)
        wb_used[slot] = kb.ct["pe"]
        if a in (17, 19, 21, 23):
            g = (a - 17) // 2
            kb.wait("scalar", "pe", wb_used[slot])
            evac4[a] = kb.op("scalar", lambda e, ps=ps, g=g: e.activation(
                vcat[:, g * T:(g + 1) * T], ps[:, :], AF.Identity), inc="ac")

    wb_used = {}
    dv4 = {}
    kb.wait("gpsimd", "pe")
    for o in range(DC):
        slot = o % 2
        prev = wb_used.get(slot)
        if prev is not None:
            kb.wait("gpsimd", "pe", prev)
        wl = kb.dma("gpsimd", wbuf4[slot].rearrange("p (c h) -> p c h", c=DC),
                    wob_h[:, o * 128:(o + 1) * 128].rearrange("(c p) h -> p c h", p=128), "gd")
        ps = PS[2 + (o % 2)]
        kb.wait("tensor", "gd", wl)
        kb.wait("tensor", "ac")
        ev = dv4.get(o - 2)
        if ev is not None:
            kb.wait("tensor", "dv", ev)
        for hfq in range(2):
            for a in range(16):
                g = a // 4
                kb.op("tensor", lambda e, ps=ps, hfq=hfq, a=a, g=g, slot=slot: e.matmul(
                    ps[:, hfq * 512:(hfq + 1) * 512],
                    wbuf4[slot][:, a * 128:(a + 1) * 128],
                    vcat[:, g * T + hfq * 512:g * T + (hfq + 1) * 512],
                    start=(a == 0), stop=(a == 15)),
                    inc="pe" if a == 15 else None)
        wb_used[slot] = kb.ct["pe"]
        kb.wait("vector", "pe", wb_used[slot])
        dv4[o] = kb.op("vector", lambda e, ps=ps, o=o: e.tensor_tensor(
            r_sb4[:, o * T:(o + 1) * T], ps[:, :], r_sb4[:, o * T:(o + 1) * T], ALU.add), inc="dv")
    kb.wait("sync", "dv")
    kb.dma("sync", r4_d.rearrange("(c p) t -> p c t", p=128),
           r_sb4.rearrange("p (c t) -> p c t", c=DC), "sy")
    kb.barrier()
    seg4.close()

    ffn_phase(r4_d, y_h)

    if MK_DEBUG:
        kb.dma("sync", dbg["r2"][:, :], r2_d[:, :], "sy")
        kb.dma("sync", dbg["r3"][:, :], r3_d[:, :], "sy")
        kb.dma("sync", dbg["r4"][:, :], r4_d[:, :], "sy")
        kb.dma("sync", dbg["gq"][:, :], gq_d[0:128, :], "sy")
        kb.dma("sync", dbg["scl"][:, :], scl[:, :], "sy")
        kb.dma("sync", dbg["sclc"][:, :], sclc[:, :], "sy")
    for s in ("sy", "sc", "gd", "t0", "r0", "r1", "s0", "s1",
              "g0", "g1", "q0", "q1", "f0", "f1", "w0"):
        kb.wait("sync", s)

    kb.emit()
    return nc, stack


def kernel(L0_attn_a_x, L0_attn_a_wq, L0_attn_a_wk, L0_attn_a_wv, L0_attn_a_wo,
           L0_attn_a_inv_freq, L0_attn_b_wqkv, L0_attn_b_wo,
           L0_ffn_gate_w, L0_ffn_gate_b, L0_ffn_val_w, L0_ffn_val_b,
           L0_ffn_out_w, L0_ffn_out_b):
    x = np.asarray(L0_attn_a_x, dtype=np.float32).reshape(S, D)
    x_t = np.ascontiguousarray(x.T)

    perm = np.concatenate([np.arange(0, HD, 2), np.arange(1, HD, 2)])
    colperm = (np.arange(D).reshape(H, HD)[:, perm]).reshape(-1)

    wq_t = np.ascontiguousarray(np.asarray(L0_attn_a_wq, np.float32).T[:, colperm])
    wk_t = np.ascontiguousarray(np.asarray(L0_attn_a_wk, np.float32).T[:, colperm])
    wv_t = np.ascontiguousarray(np.asarray(L0_attn_a_wv, np.float32).T)
    wo_t = np.ascontiguousarray(np.asarray(L0_attn_a_wo, np.float32).T)
    wqkv_t = np.ascontiguousarray(np.asarray(L0_attn_b_wqkv, np.float32).T)
    wob_t = np.ascontiguousarray(np.asarray(L0_attn_b_wo, np.float32).T)
    wg_t = np.ascontiguousarray(np.asarray(L0_ffn_gate_w, np.float32).T)
    wvl_t = np.ascontiguousarray(np.asarray(L0_ffn_val_w, np.float32).T)
    wout_t = np.ascontiguousarray(np.asarray(L0_ffn_out_w, np.float32).T)
    bg = np.ascontiguousarray(np.asarray(L0_ffn_gate_b, np.float32).reshape(FC, 128).T)
    bv = np.ascontiguousarray(np.asarray(L0_ffn_val_b, np.float32).reshape(FC, 128).T)
    bo = np.ascontiguousarray(np.asarray(L0_ffn_out_b, np.float32).reshape(DC, 128).T)
    invf1 = np.asarray(L0_attn_a_inv_freq, np.float32).reshape(64)
    invf = np.ascontiguousarray(np.concatenate([invf1, invf1]).reshape(1, 128))
    idf = np.eye(128, dtype=np.float32)
    jm = np.zeros((128, 128), dtype=np.float32)
    jm[np.arange(128), (np.arange(128) + 64) % 128] = 1.0
    tidx = np.arange(S, dtype=np.float32)
    ohs = np.zeros((128, H * 16), dtype=np.float32)
    ohr = np.zeros((16, H * 128), dtype=np.float32)
    for kh in range(H):
        ohs[:, kh * 16 + kh] = 1.0
        ohr[kh, kh * 128:(kh + 1) * 128] = 1.0

    nc, stack = build_nc()

    in_maps = []
    for c in range(NCORE):
        in_maps.append({
            "xp": np.ascontiguousarray(x_t[:, c * T:(c + 1) * T]),
            "tp": np.ascontiguousarray(tidx[c * T:(c + 1) * T]).reshape(1, T),
            "invf": invf, "wq": wq_t, "wk": wk_t, "wv": wv_t, "wo": wo_t,
            "wqkv": wqkv_t, "wob": wob_t, "wg": wg_t, "wvl": wvl_t, "wout": wout_t,
            "bg": bg, "bv": bv, "bo": bo, "idf": idf, "jm": jm,
            "ohs": ohs, "ohr": ohr,
        })
    res = run_bass_kernel_spmd(nc, in_maps, core_ids=list(range(NCORE)))
    stack.close()
    if MK_DEBUG:
        kernel.dbg_results = res.results
    y_fm = np.concatenate([np.asarray(res.results[c]["y"]) for c in range(NCORE)], axis=1)
    return np.ascontiguousarray(y_fm.T).reshape(1, S, D)
